# version 05 (final)
# baseline (speedup 1.0000x reference)
import numpy as np
import ml_dtypes
import concourse.bass as bass
import concourse.mybir as mybir
from concourse.bass_utils import run_bass_kernel_spmd

F32 = mybir.dt.float32
BF16 = mybir.dt.bfloat16
I32 = mybir.dt.int32
AF = mybir.ActivationFunctionType
ALU = mybir.AluOpType
AXX = mybir.AxisListType.X
BF = ml_dtypes.bfloat16

B, S, D = 2, 16384, 1024
NBLK, BLK = 8, 2048
T = 2 * BLK
HID0 = 3072
DI, NH, HD, DS = 2048, 32, 64, 128
DPROJ = 4384
HID1 = 4096
CH = 128
NEG = -30000.0
CORES = list(range(8))


def blocks_of(core):
    j = core % 4
    return (j, 7 - j)


class Sem:
    def __init__(self, nc, name):
        self.h = nc.alloc_semaphore(name)
        self.n = 0

    def inc(self, instr, kk=1):
        instr.then_inc(self.h, kk)
        self.n += kk
        return instr

    def wait(self, eng):
        if self.n:
            eng.wait_ge(self.h, self.n)

    def wait_at(self, eng, v):
        if v > 0:
            eng.wait_ge(self.h, v)


class Buf:

    def __init__(self):
        self.r = None
        self.w = None

    def rd(self, sem):
        self.r = (sem, sem.n)

    def wr(self, sem):
        self.w = (sem, sem.n)

    def guard_w(self, eng):
        if self.r is not None:
            self.r[0].wait_at(eng, self.r[1])

    def guard_r(self, eng):
        if self.w is not None:
            self.w[0].wait_at(eng, self.w[1])


class K:
    def __init__(self, name):
        self.nc = bass.Bass("TRN2", target_bir_lowering=False)
        self.name = name
        self.t = self.nc.tensor
        self.v = self.nc.vector
        self.s = self.nc.scalar
        self.g = self.nc.gpsimd
        self.y = self.nc.sync
        self._consts = {}
        self.SC = Sem(self.nc, "SC")
        self._cwaited = {}

    def sem(self, name):
        return Sem(self.nc, name)

    def sb(self, name, shape, dtype):
        return self.nc.alloc_sbuf_tensor(name, shape, dtype)

    def ps(self, name, shape, dtype=F32):
        return self.nc.alloc_psum_tensor(name, shape, dtype)

    def const(self, name, arr):
        if name not in self._consts:
            arr = np.ascontiguousarray(arr)
            dr = self.nc.inline_tensor(arr, name)
            md = {np.dtype(np.float32): F32, np.dtype(BF): BF16,
                  np.dtype(np.int32): I32}[arr.dtype]
            sbt = self.nc.alloc_sbuf_tensor("c_" + name, list(arr.shape), md)
            self.SC.inc(self.y.dma_start(out=sbt[:], in_=dr.ap()), 16)
            self._consts[name] = sbt
        return self._consts[name][:]

    def fconst(self, val, parts=1, dtype=F32):
        npdt = np.float32 if dtype == F32 else BF
        name = "fc_%s_%s_%d" % (str(dtype), repr(float(val)), parts)
        name = name.replace(".", "_").replace("-", "m")
        return self.const(name, np.full((parts, 1), val, dtype=npdt))

    def cwait(self, eng):
        key = id(eng)
        if self._cwaited.get(key) != self.SC.n and self.SC.n:
            eng.wait_ge(self.SC.h, self.SC.n)
            self._cwaited[key] = self.SC.n

    @property
    def ones_col_bf(self):
        return self.const("ones_col_bf", np.ones((128, 1), dtype=BF))

    @property
    def ones_row_bf(self):
        return self.const("ones_row_bf", np.ones((1, 128), dtype=BF))

    @property
    def ones_row_f32(self):
        return self.const("ones_row_f32", np.ones((1, 128), dtype=np.float32))

    @property
    def ident_bf(self):
        return self.const("ident_bf", np.eye(128, dtype=BF))

    @property
    def ident_f32(self):
        return self.const("ident_f32", np.eye(128, dtype=np.float32))


def ln_tile(k, E, xf, nt, eps, apply_out, extra_stats=None):
    t, v, s = k.t, k.v, k.s
    ST, SV, SS = E['ST'], E['SV'], E['SS']
    sq, xb, rows = E['sq'], E['xb'], E['rows']
    ps_s1, ps_q1, ps_b1, ps_b2 = E['ps_s1'], E['ps_q1'], E['ps_b1'], E['ps_b2']
    for c in range(8):
        xs = xf[:, c * nt:(c + 1) * nt]
        i1 = s.activation(sq[:, c * nt:(c + 1) * nt], xs, AF.Square)
        i2 = v.tensor_copy(xb[:, c * nt:(c + 1) * nt], xs)
    SS.inc(i1)
    SV.inc(i2)
    SS.wait(t)
    SV.wait(t)
    k.cwait(t)
    for c in range(8):
        t.matmul(ps_s1[:, 0:nt], k.ones_col_bf,
                 xb[:, c * nt:(c + 1) * nt], start=(c == 0), stop=(c == 7))
    for c in range(8):
        i1 = t.matmul(ps_q1[:, 0:nt], k.ones_col_bf,
                      sq[:, c * nt:(c + 1) * nt], start=(c == 0), stop=(c == 7))
    ST.inc(i1)
    mrow, qrow, srow, rrow, mrrow = (
        rows[0:1, i * nt:(i + 1) * nt] for i in range(5))
    ST.wait(v)
    v.tensor_scalar_mul(mrow, ps_s1[0:1, 0:nt], 1.0 / 1024.0)
    SV.inc(v.tensor_scalar_mul(qrow, ps_q1[0:1, 0:nt], 1.0 / 1024.0))
    SV.wait(v)
    SV.inc(v.tensor_tensor(srow, mrow, mrow, ALU.mult))
    SV.wait(v)
    SV.inc(v.tensor_sub(srow, qrow, srow))
    SV.wait(s)
    k.cwait(s)
    SS.inc(s.activation(srow, srow, AF.Sqrt, bias=k.fconst(eps)))
    SS.wait(v)
    SV.inc(v.reciprocal(rrow, srow))
    SV.wait(v)
    SV.inc(v.tensor_tensor(mrrow, mrow, rrow, ALU.mult))
    SV.wait(v)
    if extra_stats is not None:
        v.tensor_copy(extra_stats[0], mrow)
        SV.inc(v.tensor_copy(extra_stats[1], rrow))
    SV.wait(t)
    k.cwait(t)
    t.matmul(ps_b1[:, 0:nt], k.ones_row_f32, rrow, start=True, stop=True)
    ST.inc(t.matmul(ps_b2[:, 0:nt], k.ones_row_f32, mrrow,
                    start=True, stop=True))
    ST.wait(v)
    last = None
    for c in range(8):
        last = apply_out(c)
    SV.inc(last)


def build_p1():
    k = K("p1")
    nc = k.nc
    x0s = nc.dram_tensor("x0s", [D, T], F32, kind="ExternalInput").ap()
    wqT = nc.dram_tensor("wqT", [D, D], F32, kind="ExternalInput").ap()
    wkT = nc.dram_tensor("wkT", [D, D], F32, kind="ExternalInput").ap()
    wvT = nc.dram_tensor("wvT", [D, D], F32, kind="ExternalInput").ap()
    bq = nc.dram_tensor("bq", [D, 1], F32, kind="ExternalInput").ap()
    bk = nc.dram_tensor("bk", [D, 1], F32, kind="ExternalInput").ap()
    bvr = nc.dram_tensor("bvr", [1, D], F32, kind="ExternalInput").ap()
    xn0_d = nc.dram_tensor("xn0", [D, T], BF16, kind="ExternalOutput").ap()
    q_d = nc.dram_tensor("q", [D, T], BF16, kind="ExternalOutput").ap()
    k_d = nc.dram_tensor("kk", [D, T], BF16, kind="ExternalOutput").ap()
    v_d = nc.dram_tensor("v", [T, D], BF16, kind="ExternalOutput").ap()

    NT = 512
    NTI = T // NT
    xn_sb = k.sb("xn_sb", [128, 8 * T], BF16)
    xf = k.sb("xf", [128, 8 * NT], F32)
    sq_scr = k.sb("sq_scr", [128, 8 * NT], BF16)
    xb_scr = k.sb("xb_scr", [128, 8 * NT], BF16)
    u_scr = k.sb("u_scr", [128, NT], F32)
    rows = k.sb("rows", [1, 8 * NT], F32)
    wq_sb = k.sb("wq_sb", [128, 8 * D], BF16)
    wk_sb = k.sb("wk_sb", [128, 8 * D], BF16)
    wv_sb = k.sb("wv_sb", [128, 8 * D], BF16)
    bq_sb = k.sb("bq_sb", [128, 8], F32)
    bk_sb = k.sb("bk_sb", [128, 8], F32)
    bq32 = k.sb("bq32", [128, 8], F32)
    bv_sb = k.sb("bv_sb", [1, D], BF16)
    ev0t = k.sb("ev0", [128, NT], BF16)
    ev1t = k.sb("ev1", [128, NT], BF16)
    ps_s1 = k.ps("ps_s1", [1, NT])
    ps_q1 = k.ps("ps_q1", [1, NT])
    ps_b1 = k.ps("ps_b1", [128, NT])
    ps_b2 = k.ps("ps_b2", [128, NT])
    ps_m0 = k.ps("ps_m0", [128, NT])
    ps_m1 = k.ps("ps_m1", [128, NT])
    ps_v0 = k.ps("ps_v0", [128, NT])
    ps_v1 = k.ps("ps_v1", [128, NT])
    ev2t = k.sb("ev2", [128, NT], BF16)
    ev3t = k.sb("ev3", [128, NT], BF16)
    with nc.Block() as _blk:
        t, v, s, g, y = k.t, k.v, k.s, k.g, k.y
        SY, SG, ST, SV, SS = (k.sem(n) for n in ("SY", "SG", "ST", "SV", "SS"))
        _ = (k.ones_col_bf, k.ones_row_bf, k.ones_row_f32)
        k.cwait(t)

        for kc in range(8):
            SG.inc(g.dma_start(out=wq_sb[:, kc * D:(kc + 1) * D],
                               in_=wqT[kc * 128:(kc + 1) * 128, :]), 16)
            SG.inc(g.dma_start(out=wk_sb[:, kc * D:(kc + 1) * D],
                               in_=wkT[kc * 128:(kc + 1) * 128, :]), 16)
            SG.inc(g.dma_start(out=wv_sb[:, kc * D:(kc + 1) * D],
                               in_=wvT[kc * 128:(kc + 1) * 128, :]), 16)
            SY.inc(y.dma_start(out=bq_sb[:, kc:kc + 1],
                               in_=bq[kc * 128:(kc + 1) * 128, :]), 16)
            SY.inc(y.dma_start(out=bk_sb[:, kc:kc + 1],
                               in_=bk[kc * 128:(kc + 1) * 128, :]), 16)
        SG.inc(g.dma_start(out=bv_sb[:], in_=bvr), 16)
        SY.wait(v)
        SV.inc(v.tensor_scalar_mul(bq32[:], bq_sb[:], 1.0 / 32.0))

        E = {'ST': ST, 'SV': SV, 'SS': SS, 'sq': sq_scr, 'xb': xb_scr,
             'rows': rows, 'ps_s1': ps_s1, 'ps_q1': ps_q1,
             'ps_b1': ps_b1, 'ps_b2': ps_b2}
        xf_buf = Buf()
        for ti in range(NTI):
            xf_buf.guard_w(y)
            for c in range(8):
                SY.inc(y.dma_start(out=xf[:, c * NT:(c + 1) * NT],
                                   in_=x0s[c * 128:(c + 1) * 128,
                                           ti * NT:(ti + 1) * NT]), 16)
            SY.wait(s)
            SY.wait(v)

            def apply(c, ti=ti):
                vv = k.v
                SV.inc(vv.tensor_tensor(u_scr[:, 0:NT],
                                        xf[:, c * NT:(c + 1) * NT],
                                        ps_b1[:, 0:NT], ALU.mult))
                SV.wait(vv)
                return vv.tensor_tensor(
                    xn_sb[:, c * T + ti * NT: c * T + (ti + 1) * NT],
                    u_scr[:, 0:NT], ps_b2[:, 0:NT], ALU.subtract)

            ln_tile(k, E, xf, NT, 1e-5, lambda c: apply(c))
            xf_buf.rd(SV)
        SV.wait(y)
        for c in range(8):
            SY.inc(y.dma_start(out=xn0_d[c * 128:(c + 1) * 128, :],
                               in_=xn_sb[:, c * T:(c + 1) * T]), 16)

        SG.wait(t)
        SV.wait(t)
        SV.wait(s)
        SY.wait(s)
        psl = [ps_m0, ps_m1]
        evl = [ev0t, ev1t]
        pbuf = [Buf(), Buf()]
        ebuf = [Buf(), Buf()]
        it = 0
        for (wsb, bias_ap, scale, outd) in ((wq_sb, bq32, 1.0 / 32.0, q_d),
                                            (wk_sb, bk_sb, 1.0, k_d)):
            for oc in range(8):
                for ti in range(NTI):
                    pp = psl[it % 2]
                    pb = pbuf[it % 2]
                    eb = ebuf[it % 2]
                    pb.guard_w(t)
                    for kc in range(8):
                        ins = t.matmul(
                            pp[:, 0:NT],
                            wsb[:, kc * D + oc * 128: kc * D + (oc + 1) * 128],
                            xn_sb[:, kc * T + ti * NT: kc * T + (ti + 1) * NT],
                            start=(kc == 0), stop=(kc == 7))
                    ST.inc(ins)
                    pb.wr(ST)
                    eb.guard_w(s)
                    pb.guard_r(s)
                    ins = s.activation(evl[it % 2][:], pp[:, 0:NT], AF.Identity,
                                       bias=bias_ap[:, oc:oc + 1], scale=scale)
                    SS.inc(ins)
                    pb.rd(SS)
                    eb.wr(SS)
                    eb.guard_r(y)
                    SY.inc(y.dma_start(
                        out=outd[oc * 128:(oc + 1) * 128, ti * NT:(ti + 1) * NT],
                        in_=evl[it % 2][:]), 16)
                    eb.rd(SY)
                    it += 1

        SG.wait(t)
        k.cwait(t)
        psl = [ps_v0, ps_v1]
        evl = [ev2t, ev3t]
        pbuf = [Buf(), Buf()]
        ebuf = [Buf(), Buf()]
        it = 0
        for ts in range(T // 128):
            for dh in range(2):
                pp = psl[it % 2]
                pb = pbuf[it % 2]
                eb = ebuf[it % 2]
                pb.guard_w(t)
                t.matmul(pp[:, 0:NT], k.ones_row_bf,
                         bv_sb[0:1, dh * NT:(dh + 1) * NT],
                         start=True, stop=False)
                for kc in range(8):
                    ins = t.matmul(
                        pp[:, 0:NT],
                        xn_sb[:, kc * T + ts * 128: kc * T + (ts + 1) * 128],
                        wv_sb[:, kc * D + dh * NT: kc * D + (dh + 1) * NT],
                        start=False, stop=(kc == 7))
                ST.inc(ins)
                pb.wr(ST)
                eb.guard_w(s)
                pb.guard_r(s)
                SS.inc(s.activation(evl[it % 2][:], pp[:, 0:NT], AF.Identity))
                pb.rd(SS)
                eb.wr(SS)
                eb.guard_r(y)
                SY.inc(y.dma_start(
                    out=v_d[ts * 128:(ts + 1) * 128, dh * NT:(dh + 1) * NT],
                    in_=evl[it % 2][:]), 16)
                eb.rd(SY)
                it += 1

        SY.wait(y)
        SG.wait(y)
    return nc


def _fm(x):
    return np.ascontiguousarray(np.swapaxes(x, -1, -2))


def _slices_fm(xfm_b, core):
    j0, j1 = blocks_of(core)
    return np.ascontiguousarray(np.concatenate(
        [xfm_b[:, j0 * BLK:(j0 + 1) * BLK], xfm_b[:, j1 * BLK:(j1 + 1) * BLK]],
        axis=1))


def _scatter_tokens(full_b, core_arr, core, axis):
    j0, j1 = blocks_of(core)
    if axis == 0:
        full_b[j0 * BLK:(j0 + 1) * BLK] = core_arr[:BLK]
        full_b[j1 * BLK:(j1 + 1) * BLK] = core_arr[BLK:]
    else:
        full_b[:, j0 * BLK:(j0 + 1) * BLK] = core_arr[:, :BLK]
        full_b[:, j1 * BLK:(j1 + 1) * BLK] = core_arr[:, BLK:]


def run_p1(inputs):
    x = inputs["L0_attn_x"]
    xfm = _fm(x)
    wqT = np.ascontiguousarray(inputs["L0_attn_Wq"].T)
    wkT = np.ascontiguousarray(inputs["L0_attn_Wk"].T)
    wvT = np.ascontiguousarray(inputs["L0_attn_Wv"].T)
    bq = np.ascontiguousarray(inputs["L0_attn_bq"].reshape(D, 1))
    bk = np.ascontiguousarray(inputs["L0_attn_bk"].reshape(D, 1))
    bvr = np.ascontiguousarray(inputs["L0_attn_bv"].reshape(1, D))
    nc = build_p1()
    in_maps = []
    for c in CORES:
        in_maps.append({
            "x0s": _slices_fm(xfm[c // 4], c),
            "wqT": wqT, "wkT": wkT, "wvT": wvT,
            "bq": bq, "bk": bk, "bvr": bvr,
        })
    res = run_bass_kernel_spmd(nc, in_maps, core_ids=CORES)
    return res.results


def kernel(**inputs):
    p1 = run_p1(inputs)
    p2 = run_p2(inputs, p1)
    p3 = run_p3(inputs, p1, p2)
    p4 = run_p4(inputs, p3)
    p4b = run_p4b(inputs, p3, p4)
    p5 = run_p5(inputs, p2, p3, p4, p4b)
    out = np.empty((B, S, D), dtype=np.float32)
    for c in CORES:
        b = c // 4
        for bi, blk in enumerate(blocks_of(c)):
            out[b, blk * BLK:(blk + 1) * BLK, :] = \
                p5[c]["x2"][:, bi * BLK:(bi + 1) * BLK].T
    return out


def _mask_np(d):
    i = np.arange(128)[:, None]
    j = np.arange(512)[None, :]
    return np.where(j <= d + i, 0.0, NEG).astype(np.float32)


def build_p2():
    k = K("p2")
    nc = k.nc
    q_d = nc.dram_tensor("q", [D, T], BF16, kind="ExternalInput").ap()
    kf = nc.dram_tensor("kf", [D, S], BF16, kind="ExternalInput").ap()
    vf = nc.dram_tensor("vf", [S, D], BF16, kind="ExternalInput").ap()
    jj = nc.dram_tensor("jj", [1, 1], I32, kind="ExternalInput").ap()
    o_d = nc.dram_tensor("o", [T, D], F32, kind="ExternalOutput").ap()
    dbg_d = nc.dram_tensor("dbg", [128, 80], F32, kind="ExternalOutput").ap()
    qsb = k.sb("qsb", [128, 8 * 1024], BF16)
    ksb = k.sb("ksb", [128, 8 * 1024], BF16)
    vsb = k.sb("vsb", [128, 8 * 1024], BF16)
    osb = k.sb("osb", [128, 8 * 1024], F32)
    probs = k.sb("probs", [128, 1024], BF16)
    probsT = k.sb("probsT", [128, 1024], BF16)
    msc = k.sb("msc", [128, 1024], F32)
    mrow = k.sb("mrow", [128, 16], F32)
    lrow = k.sb("lrow", [128, 8], F32)
    wrow = k.sb("wrow", [128, 12], F32)
    ps_sc0 = k.ps("ps_sc0", [128, 512])
    ps_sc1 = k.ps("ps_sc1", [128, 512])
    ps_pt = k.ps("ps_pt", [128, 1024], BF16)
    ps_pv0 = k.ps("ps_pv0", [128, 512])
    ps_pv1 = k.ps("ps_pv1", [128, 512])

    return k, (q_d, kf, vf, o_d, dbg_d, qsb, ksb, vsb, osb, probs, probsT,
               msc, mrow, lrow, wrow, ps_sc0, ps_sc1, ps_pt, ps_pv0, ps_pv1)


def build_p2_for_j(j, dbg=False, serial=False):
    k, H = build_p2()
    (q_d, kf, vf, o_d, dbg_d, qsb, ksb, vsb, osb, probs, probsT, msc,
     mrow, lrow, wrow, ps_sc0, ps_sc1, ps_pt, ps_pv0, ps_pv1) = H
    nc = k.nc
    t, v, s, g, y = k.t, k.v, k.s, k.g, k.y
    SY, SG, ST, SV, SS = (k.sem(n) for n in ("SY", "SG", "ST", "SV", "SS"))
    j0, j1 = j, 7 - j

    with nc.Block() as _blk:
        masks = {d: k.const("mask_%d" % d, _mask_np(d)) for d in
                 (0, 128, 256, 384)}
        idb = k.ident_bf
        k.cwait(t)
        k.cwait(v)
        osb_buf = Buf()
        kv_buf = Buf()
        q_buf = Buf()
        for grp in range(4):
            blkno = j0 if grp < 2 else j1
            q0 = blkno * BLK + (grp % 2) * 1024
            osb_buf.guard_w(g)
            SG.inc(g.memset(osb[:], 0.0))
            SG.inc(g.memset(mrow[:, 0:8], NEG))
            SG.inc(g.memset(lrow[:], 0.0))
            q_buf.guard_w(y)
            for c in range(8):
                SY.inc(y.dma_start(
                    out=qsb[:, c * 1024:(c + 1) * 1024],
                    in_=q_d[c * 128:(c + 1) * 128,
                            grp * 1024:(grp + 1) * 1024]), 16)
            SY.wait(t)
            SG.wait(v)
            SG.wait(s)
            for sblk in range(q0 // 1024 + 1):
                kv_buf.guard_w(y)
                for c in range(8):
                    SY.inc(y.dma_start(
                        out=ksb[:, c * 1024:(c + 1) * 1024],
                        in_=kf[c * 128:(c + 1) * 128,
                               sblk * 1024:(sblk + 1) * 1024]), 16)
                for kc in range(8):
                    SY.inc(y.dma_start(
                        out=vsb[:, kc * 1024:(kc + 1) * 1024],
                        in_=vf[sblk * 1024 + kc * 128:
                               sblk * 1024 + (kc + 1) * 128, :]), 16)
                SY.wait(t)
                for qt in range(8):
                    q0g = q0 + qt * 128
                    halves = []
                    for nh in range(2):
                        k0h = sblk * 1024 + nh * 512
                        dd = q0g - k0h
                        if k0h > q0g + 127:
                            continue
                        halves.append((nh, dd, dd < 512))
                    if not halves:
                        continue
                    pss = [ps_sc0, ps_sc1]
                    SS.wait(t)
                    SV.wait(t)
                    for (nh, dd, partial) in halves:
                        for kc in range(8):
                            ins = t.matmul(
                                pss[nh][:],
                                qsb[:, kc * 1024 + qt * 128:
                                    kc * 1024 + (qt + 1) * 128],
                                ksb[:, kc * 1024 + nh * 512:
                                    kc * 1024 + (nh + 1) * 512],
                                start=(kc == 0), stop=(kc == 7))
                    ST.inc(ins)
                    ST.wait(v)
                    did_mask = False
                    for (nh, dd, partial) in halves:
                        if partial:
                            ins = v.tensor_tensor(
                                msc[:, nh * 512:(nh + 1) * 512],
                                pss[nh][:], masks[dd], ALU.add)
                            did_mask = True
                    if did_mask:
                        SV.inc(ins)
                        SV.wait(v)
                    cur = mrow[:, qt:qt + 1]
                    v.tensor_copy(wrow[:, 0:1], cur)
                    ncand = 1
                    for (nh, dd, partial) in halves:
                        src_ = msc[:, nh * 512:(nh + 1) * 512] if partial \
                            else pss[nh][:]
                        ins = v.reduce_max(wrow[:, ncand:ncand + 1], src_, AXX)
                        ncand += 1
                    SV.inc(ins)
                    SV.wait(v)
                    mn = wrow[:, 3:4]
                    negm = wrow[:, 4:5]
                    resc = wrow[:, 5:6]
                    v.reduce_max(mn, wrow[:, 0:ncand], AXX)
                    SV.inc(v.reduce_max(negm, wrow[:, 0:ncand], AXX,
                                        negate=True))
                    SV.wait(v)
                    SV.wait(s)
                    s.activation(resc, cur, AF.Exp, bias=negm)
                    rs = [wrow[:, 6:7], wrow[:, 7:8]]
                    for i, (nh, dd, partial) in enumerate(halves):
                        src_ = msc[:, nh * 512:(nh + 1) * 512] if partial \
                            else pss[nh][:]
                        ins = s.activation(probs[:, nh * 512:(nh + 1) * 512],
                                           src_, AF.Exp, bias=negm,
                                           accum_out=rs[i])
                    SS.inc(ins)
                    SS.wait(v)
                    v.tensor_copy(cur, mn)
                    rsum = wrow[:, 8:9]
                    if len(halves) == 2:
                        ins = v.tensor_tensor(rsum, rs[0], rs[1], ALU.add)
                    else:
                        ins = v.tensor_copy(rsum, rs[0])
                    SV.inc(ins)
                    SV.wait(v)
                    lcur = lrow[:, qt:qt + 1]
                    SV.inc(v.scalar_tensor_tensor(lcur, lcur, resc, rsum,
                                                  ALU.mult, ALU.add))
                    SV.wait(v)
                    kcs = [kc for (nh, _, _) in halves
                           for kc in range(nh * 4, nh * 4 + 4)]
                    SS.wait(t)
                    for kc in kcs:
                        ins = t.transpose(ps_pt[:, kc * 128:(kc + 1) * 128],
                                          probs[:, kc * 128:(kc + 1) * 128],
                                          idb)
                    ST.inc(ins)
                    ST.wait(s)
                    SS.inc(s.activation(probsT[:], ps_pt[:], AF.Identity))
                    SS.wait(t)
                    SV.wait(t)
                    pvs = [ps_pv0, ps_pv1]
                    for dh in range(2):
                        for i, kc in enumerate(kcs):
                            ins = t.matmul(
                                pvs[dh][:],
                                probsT[:, kc * 128:(kc + 1) * 128],
                                vsb[:, kc * 1024 + dh * 512:
                                    kc * 1024 + (dh + 1) * 512],
                                start=(i == 0), stop=(i == len(kcs) - 1))
                    ST.inc(ins)
                    ST.wait(v)
                    for dh in range(2):
                        osl = osb[:, qt * 1024 + dh * 512:
                                  qt * 1024 + (dh + 1) * 512]
                        ins = v.scalar_tensor_tensor(osl, osl, resc,
                                                     pvs[dh][:],
                                                     ALU.mult, ALU.add)
                    SV.inc(ins)
                    SV.wait(v)
                kv_buf.rd(ST)
            if dbg:
                SV.wait(y)
                SY.inc(y.dma_start(out=dbg_d[:, grp * 16:grp * 16 + 8],
                                   in_=lrow[:, 0:8]), 16)
                SY.inc(y.dma_start(out=dbg_d[:, grp * 16 + 8:grp * 16 + 16],
                                   in_=mrow[:, 0:8]), 16)
            linv = wrow[:, 9:10]
            for qt in range(8):
                SV.inc(v.reciprocal(linv, lrow[:, qt:qt + 1]))
                SV.wait(v)
                ins = v.tensor_scalar_mul(
                    osb[:, qt * 1024:(qt + 1) * 1024],
                    osb[:, qt * 1024:(qt + 1) * 1024], linv)
                SV.inc(ins)
                SV.wait(y)
                SY.inc(y.dma_start(
                    out=o_d[grp * 1024 + qt * 128:grp * 1024 + (qt + 1) * 128, :],
                    in_=osb[:, qt * 1024:(qt + 1) * 1024]), 16)
            osb_buf.rd(SY)
            q_buf.rd(ST)
            for _sem in (SY, SV, ST, SS, SG):
                for _eng in (t, v, s, g, y):
                    _sem.wait(_eng)
        SY.wait(y)
        SG.wait(y)
    return k.nc


def run_p2(inputs, p1res):
    kfull = np.zeros((B, D, S), dtype=BF)
    vfull = np.zeros((B, S, D), dtype=BF)
    for c in CORES:
        b = c // 4
        _scatter_tokens(kfull[b], p1res[c]["kk"], c, axis=1)
        _scatter_tokens(vfull[b], p1res[c]["v"], c, axis=0)
    ncs = {}
    res = [None] * 8
    for j in range(4):
        ncs[j] = build_p2_for_j(j)
    for j in range(4):
        cores = [j, j + 4]
        in_maps = [{"q": p1res[c]["q"], "kf": kfull[c // 4],
                    "vf": vfull[c // 4],
                    "jj": np.zeros((1, 1), np.int32)} for c in cores]
        r = run_bass_kernel_spmd(ncs[j], in_maps, core_ids=cores)
        for i, c in enumerate(cores):
            res[c] = r.results[i]
    return res


def build_p3():
    k = K("p3")
    nc = k.nc
    x0s = nc.dram_tensor("x0s", [D, T], F32, kind="ExternalInput").ap()
    at0 = nc.dram_tensor("at0", [D, T], F32, kind="ExternalInput").ap()
    xn0 = nc.dram_tensor("xn0", [D, T], BF16, kind="ExternalInput").ap()
    wg_img = nc.dram_tensor("wg_img", [24, 128, 1024], I32,
                            kind="ExternalInput").ap()
    wu_img = nc.dram_tensor("wu_img", [24, 128, 1024], I32,
                            kind="ExternalInput").ap()
    wd_img = nc.dram_tensor("wd_img", [8, 128, 3072], I32,
                            kind="ExternalInput").ap()
    nwr = nc.dram_tensor("nwr", [D, 1], F32, kind="ExternalInput").ap()
    x1_d = nc.dram_tensor("x1", [D, T], F32, kind="ExternalOutput").ap()
    st1_d = nc.dram_tensor("st1", [2, T], F32, kind="ExternalOutput").ap()
    xn1_d = nc.dram_tensor("xn1", [D, T], BF16, kind="ExternalOutput").ap()
    xr_d = nc.dram_tensor("xr", [D, T], BF16, kind="ExternalOutput").ap()

    NT = 512
    NTI = T // NT
    SC16 = 1.0 / (127.0 * 127.0)
    xnt = k.sb("xnt", [128, 8 * NT], BF16)
    xat = k.sb("xat", [128, NT], F32)
    aat = k.sb("aat", [128, NT], F32)
    x0a = k.sb("x0a", [128, 8 * NT], F32)
    x1t = k.sb("x1t", [128, 8 * NT], F32)
    hsb = k.sb("hsb", [128, 24 * NT], BF16)
    gev = k.sb("gev", [128, NT], BF16)
    uev = k.sb("uev", [128, NT], BF16)
    usc = k.sb("usc", [128, 8 * NT], BF16)
    xr_t = k.sb("xr_t", [128, 8 * NT], BF16)
    xn1t = k.sb("xn1t", [128, 8 * NT], BF16)
    sq_scr = k.sb("sq_scr", [128, 8 * NT], BF16)
    xb_scr = k.sb("xb_scr", [128, 8 * NT], BF16)
    u_scr = k.sb("u_scr", [128, NT], F32)
    rows = k.sb("rows", [1, 8 * NT], F32)
    nw_sb = k.sb("nw_sb", [128, 8], F32)
    stm = k.sb("stm", [1, NT], F32)
    str_ = k.sb("str", [1, NT], F32)
    wga = k.sb("wga", [128, 1024], BF16)
    wgb = k.sb("wgb", [128, 1024], BF16)
    wua = k.sb("wua", [128, 1024], BF16)
    wub = k.sb("wub", [128, 1024], BF16)
    wda = k.sb("wda", [128, 3072], BF16)
    wdb = k.sb("wdb", [128, 3072], BF16)
    ps_s1 = k.ps("ps_s1", [1, NT])
    ps_q1 = k.ps("ps_q1", [1, NT])
    ps_b1 = k.ps("ps_b1", [128, NT])
    ps_b2 = k.ps("ps_b2", [128, NT])
    ps_a = k.ps("ps_a", [128, NT])
    ps_b = k.ps("ps_b", [128, NT])

    with nc.Block() as _blk:
        t, v, s, g, y = k.t, k.v, k.s, k.g, k.y
        SY, SG, ST, SV, SS = (k.sem(n) for n in ("SY", "SG", "ST", "SV", "SS"))
        _ = (k.ones_col_bf, k.ones_row_f32)
        k.cwait(t)
        E = {'ST': ST, 'SV': SV, 'SS': SS, 'sq': sq_scr, 'xb': xb_scr,
             'rows': rows, 'ps_s1': ps_s1, 'ps_q1': ps_q1,
             'ps_b1': ps_b1, 'ps_b2': ps_b2}
        for c in range(8):
            SY.inc(y.dma_start(out=nw_sb[:, c:c + 1],
                               in_=nwr[c * 128:(c + 1) * 128, :]), 16)
        wbufs = [(wga, wua), (wgb, wub)]
        wdbufs = [wda, wdb]
        wbuf_state = [Buf(), Buf()]
        wdbuf_state = [Buf(), Buf()]
        in_bufs = Buf()
        ln_outs = Buf()
        stat_buf = Buf()
        psab = [Buf(), Buf()]
        gu_ev = Buf()

        for ti in range(NTI):
            in_bufs.guard_w(y)
            for c in range(8):
                SY.inc(y.dma_start(out=xnt[:, c * NT:(c + 1) * NT],
                                   in_=xn0[c * 128:(c + 1) * 128,
                                           ti * NT:(ti + 1) * NT]), 16)
            ldmark = SY.n
            for c in range(8):
                SY.inc(y.dma_start(out=xat[:], in_=x0s[
                    c * 128:(c + 1) * 128, ti * NT:(ti + 1) * NT]), 16)
                SY.inc(y.dma_start(out=aat[:], in_=at0[
                    c * 128:(c + 1) * 128, ti * NT:(ti + 1) * NT]), 16)
                SY.wait(v)
                SV.inc(v.tensor_tensor(x0a[:, c * NT:(c + 1) * NT],
                                       xat[:], aat[:], ALU.add))
                SV.wait(y)
            for hc in range(24):
                wsl = wbufs[hc % 2]
                wst = wbuf_state[hc % 2]
                wst.guard_w(g)
                SG.inc(g.dma_start(out=wsl[0][:], in_=wg_img[hc]), 16)
                SG.inc(g.dma_start(out=wsl[1][:], in_=wu_img[hc]), 16)
                pb = psab[0]
                pb.guard_w(t)
                SG.wait(t)
                if hc == 0:
                    SY.wait_at(t, ldmark)
                for kc in range(8):
                    ins = t.matmul(ps_a[:],
                                   wsl[0][:, kc * 128:(kc + 1) * 128],
                                   xnt[:, kc * NT:(kc + 1) * NT],
                                   start=(kc == 0), stop=(kc == 7))
                for kc in range(8):
                    ins2 = t.matmul(ps_b[:],
                                    wsl[1][:, kc * 128:(kc + 1) * 128],
                                    xnt[:, kc * NT:(kc + 1) * NT],
                                    start=(kc == 0), stop=(kc == 7))
                ST.inc(ins)
                ST.inc(ins2)
                wst.rd(ST)
                gu_ev.guard_w(s)
                ST.wait(s)
                s.activation(gev[:], ps_a[:], AF.Silu, scale=1.0 / 127.0)
                SS.inc(s.activation(uev[:], ps_b[:], AF.Identity))
                pb.rd(SS)
                SS.wait(v)
                SV.inc(v.tensor_tensor(hsb[:, hc * NT:(hc + 1) * NT],
                                       gev[:], uev[:], ALU.mult))
                gu_ev.rd(SV)
            SV.wait(t)
            SY.wait(v)
            SS.wait(v)
            for oc in range(8):
                wd_sl = wdbufs[oc % 2]
                wd_st = wdbuf_state[oc % 2]
                wd_st.guard_w(g)
                SG.inc(g.dma_start(out=wd_sl[:], in_=wd_img[oc]), 16)
                pb = psab[1]
                pb.guard_w(t)
                SG.wait(t)
                for hc in range(24):
                    ins = t.matmul(ps_b[:],
                                   wd_sl[:, hc * 128:(hc + 1) * 128],
                                   hsb[:, hc * NT:(hc + 1) * NT],
                                   start=(hc == 0), stop=(hc == 23))
                ST.inc(ins)
                wd_st.rd(ST)
                ST.wait(v)
                SV.inc(v.scalar_tensor_tensor(
                    x1t[:, oc * NT:(oc + 1) * NT], ps_b[:], SC16,
                    x0a[:, oc * NT:(oc + 1) * NT], ALU.mult, ALU.add))
                pb.rd(SV)
            in_bufs.rd(SV)
            SV.wait(y)
            for c in range(8):
                SY.inc(y.dma_start(out=x1_d[c * 128:(c + 1) * 128,
                                            ti * NT:(ti + 1) * NT],
                                   in_=x1t[:, c * NT:(c + 1) * NT]), 16)
            stat_buf.guard_w(v)
            SV.wait(s)
            def apply(c, ti=ti):
                SV.inc(v.tensor_tensor(u_scr[:, 0:NT],
                                       x1t[:, c * NT:(c + 1) * NT],
                                       ps_b1[:, 0:NT], ALU.mult))
                SV.wait(v)
                return v.tensor_tensor(xn1t[:, c * NT:(c + 1) * NT],
                                       u_scr[:, 0:NT], ps_b2[:, 0:NT],
                                       ALU.subtract)
            ln_tile(k, E, x1t, NT, 1e-5, apply,
                    extra_stats=(stm[0:1, 0:NT], str_[0:1, 0:NT]))
            SV.wait(y)
            SY.inc(y.dma_start(out=st1_d[0:1, ti * NT:(ti + 1) * NT],
                               in_=stm[0:1, 0:NT]), 16)
            SY.inc(y.dma_start(out=st1_d[1:2, ti * NT:(ti + 1) * NT],
                               in_=str_[0:1, 0:NT]), 16)
            stat_buf.rd(SY)
            for c in range(8):
                SY.inc(y.dma_start(out=xn1_d[c * 128:(c + 1) * 128,
                                             ti * NT:(ti + 1) * NT],
                                   in_=xn1t[:, c * NT:(c + 1) * NT]), 16)
            SV.wait(s)
            for c in range(8):
                ins = s.activation(sq_scr[:, c * NT:(c + 1) * NT],
                                   xn1t[:, c * NT:(c + 1) * NT], AF.Square)
            SS.inc(ins)
            SS.wait(t)
            SV.wait(t)
            for c in range(8):
                ins = t.matmul(ps_s1[:, 0:NT], k.ones_col_bf,
                               sq_scr[:, c * NT:(c + 1) * NT],
                               start=(c == 0), stop=(c == 7))
            ST.inc(ins)
            ST.wait(v)
            r2 = rows[0:1, 5 * NT:6 * NT]
            SV.inc(v.tensor_scalar_mul(r2, ps_s1[0:1, 0:NT], 1.0 / 1024.0))
            SV.wait(s)
            SS.inc(s.activation(r2, r2, AF.Sqrt, bias=k.fconst(1e-6)))
            SS.wait(v)
            SV.inc(v.reciprocal(r2, r2))
            SV.wait(t)
            ST.inc(t.matmul(ps_b1[:, 0:NT], k.ones_row_f32, r2,
                            start=True, stop=True))
            SY.wait(v)
            for c in range(8):
                ins = v.tensor_scalar_mul(usc[:, c * NT:(c + 1) * NT],
                                          xnt[:, 0:0] if False else
                                          xn1t[:, c * NT:(c + 1) * NT],
                                          nw_sb[:, c:c + 1])
            SV.inc(ins)
            SV.wait(v)
            ST.wait(v)
            for c in range(8):
                ins = v.tensor_tensor(xr_t[:, c * NT:(c + 1) * NT],
                                      usc[:, c * NT:(c + 1) * NT],
                                      ps_b1[:, 0:NT], ALU.mult)
            SV.inc(ins)
            ln_outs.rd(SV)
            SV.wait(y)
            for c in range(8):
                SY.inc(y.dma_start(out=xr_d[c * 128:(c + 1) * 128,
                                            ti * NT:(ti + 1) * NT],
                                   in_=xr_t[:, c * NT:(c + 1) * NT]), 16)
            for _sem in (SY, SV, ST, SS, SG):
                for _eng in (t, v, s, g, y):
                    _sem.wait(_eng)
        SY.wait(y)
        SG.wait(y)
    return nc


def run_p3(inputs, p1res, p2res):
    xfm = _fm(inputs["L0_attn_x"])
    at_full = np.zeros((B, S, D), dtype=np.float32)
    for c in CORES:
        _scatter_tokens(at_full[c // 4], p2res[c]["o"], c, axis=0)
    at_fm = _fm(at_full)
    wg = inputs["L0_ffn_w_gate"]
    wu = inputs["L0_ffn_w_up"]
    wd = inputs["L0_ffn_w_down"]
    def img_outer(w, nhc):
        o, i = w.shape
        a = w.reshape(nhc, 128, i // 128, 128)
        return np.ascontiguousarray(a.transpose(0, 3, 2, 1)).reshape(
            nhc, 128, i)
    wg_img = img_outer(wg, 24)
    wu_img = img_outer(wu, 24)
    wd_img = img_outer(wd, 8)
    nwr = np.ascontiguousarray(
        inputs["L1_attn_norm_w"].reshape(D, 1)).astype(np.float32)
    nc = build_p3()
    in_maps = []
    for c in CORES:
        b = c // 4
        in_maps.append({
            "x0s": _slices_fm(xfm[b], c),
            "at0": _slices_fm(at_fm[b], c),
            "xn0": p1res[c]["xn0"],
            "wg_img": wg_img, "wu_img": wu_img, "wd_img": wd_img,
            "nwr": nwr,
        })
    res = run_bass_kernel_spmd(nc, in_maps, core_ids=CORES)
    return res.results


def build_p4():
    k = K("p4")
    nc = k.nc
    xr = nc.dram_tensor("xr", [D, T], BF16, kind="ExternalInput").ap()
    xn1 = nc.dram_tensor("xn1", [D, T], BF16, kind="ExternalInput").ap()
    ipw = nc.dram_tensor("ipw_img", [34, 128, D], F32, kind="ExternalInput").ap()
    dtw = nc.dram_tensor("dtw_img", [128, 256], F32, kind="ExternalInput").ap()
    w1i = nc.dram_tensor("w1_img", [32, 128, D], F32, kind="ExternalInput").ap()
    w2i = nc.dram_tensor("w2_img", [32, 128, D], F32, kind="ExternalInput").ap()
    h1i = nc.dram_tensor("h1_img", [32, 128, HID1], F32, kind="ExternalInput").ap()
    h2i = nc.dram_tensor("h2_img", [32, 128, HID1], F32, kind="ExternalInput").ap()
    w3i = nc.dram_tensor("w3_img", [8, 128, HID1], F32, kind="ExternalInput").ap()
    s1r = nc.dram_tensor("s1r", [HID1, 1], F32, kind="ExternalInput").ap()
    s2r = nc.dram_tensor("s2r", [HID1, 1], F32, kind="ExternalInput").ap()
    sh1r = nc.dram_tensor("sh1r", [HID1, 1], F32, kind="ExternalInput").ap()
    sh2r = nc.dram_tensor("sh2r", [HID1, 1], F32, kind="ExternalInput").ap()
    s3r = nc.dram_tensor("s3r", [D, 1], F32, kind="ExternalInput").ap()
    dtb = nc.dram_tensor("dtb", [1, NH], F32, kind="ExternalInput").ap()
    z_d = nc.dram_tensor("z", [DI, T], BF16, kind="ExternalOutput").ap()
    xs_d = nc.dram_tensor("xs", [DI, T], BF16, kind="ExternalOutput").ap()
    B_d = nc.dram_tensor("Bf", [DS, T], BF16, kind="ExternalOutput").ap()
    C_d = nc.dram_tensor("Cf", [DS, T], BF16, kind="ExternalOutput").ap()
    dtT_d = nc.dram_tensor("dtT", [T, NH], F32, kind="ExternalOutput").ap()
    f1_d = nc.dram_tensor("f1", [D, T], F32, kind="ExternalOutput").ap()

    NT = 512
    NTI = T // NT
    FT = 1024
    FTI = T // FT
    xr_t = k.sb("xr_t", [128, 8 * NT], BF16)
    xnt = k.sb("xnt", [128, 8 * FT], BF16)
    wa = k.sb("wa", [128, D], BF16)
    wb = k.sb("wb", [128, D], BF16)
    wha = k.sb("wha", [128, HID1], BF16)
    whb = k.sb("whb", [128, HID1], BF16)
    dtw_sb = k.sb("dtw_sb", [128, 256], BF16)
    dtbb = k.sb("dtbb", [128, NH], F32)
    ev0 = k.sb("ev0", [128, NT], BF16)
    ev1 = k.sb("ev1", [128, NT], BF16)
    gsb = k.sb("gsb", [128, FT], BF16)
    usb = k.sb("usb", [128, FT], BF16)
    t1sb = k.sb("t1sb", [128, 32 * FT], BF16)
    t2sb = k.sb("t2sb", [128, 32 * FT], BF16)
    fev = k.sb("fev", [128, FT], F32)
    dsc = k.sb("dsc", [128, 8 * NH], F32)
    dto = k.sb("dto", [128, NH], F32)
    sc_sb = k.sb("sc_sb", [128, 8 * 5], F32)
    ps_a = k.ps("ps_a", [128, NT])
    ps_b = k.ps("ps_b", [128, NT])
    ps_c = k.ps("ps_c", [128, NT])
    ps_d = k.ps("ps_d", [128, NT])
    ps_dt = k.ps("ps_dt", [128, NH])
    ps_bc = k.ps("ps_bc", [128, NH])

    with nc.Block() as _blk:
        t, v, s, g, y = k.t, k.v, k.s, k.g, k.y
        SY, SG, ST, SV, SS = (k.sem(n) for n in ("SY", "SG", "ST", "SV", "SS"))
        _ = (k.ones_row_f32, k.fconst(1.0, 128))
        k.cwait(t)
        k.cwait(s)
        s1sb = k.sb("s1sb", [128, 32], F32)
        s2sb = k.sb("s2sb", [128, 32], F32)
        sh1sb = k.sb("sh1sb", [128, 32], F32)
        sh2sb = k.sb("sh2sb", [128, 32], F32)
        s3sb = k.sb("s3sb", [128, 8], F32)
        for oc in range(32):
            SY.inc(y.dma_start(out=s1sb[:, oc:oc + 1],
                               in_=s1r[oc * 128:(oc + 1) * 128, :]), 16)
            SY.inc(y.dma_start(out=s2sb[:, oc:oc + 1],
                               in_=s2r[oc * 128:(oc + 1) * 128, :]), 16)
            SY.inc(y.dma_start(out=sh1sb[:, oc:oc + 1],
                               in_=sh1r[oc * 128:(oc + 1) * 128, :]), 16)
            SY.inc(y.dma_start(out=sh2sb[:, oc:oc + 1],
                               in_=sh2r[oc * 128:(oc + 1) * 128, :]), 16)
        for oc in range(8):
            SY.inc(y.dma_start(out=s3sb[:, oc:oc + 1],
                               in_=s3r[oc * 128:(oc + 1) * 128, :]), 16)
        SG.inc(g.dma_start(out=dtw_sb[:], in_=dtw), 16)
        dtb_sb = k.sb("dtb_sb", [1, NH], F32)
        SY.inc(y.dma_start(out=dtb_sb[:], in_=dtb), 16)
        SY.wait(t)
        ST.inc(t.matmul(ps_bc[:], k.ones_row_f32, dtb_sb[:],
                        start=True, stop=True))
        ST.wait(v)
        SV.inc(v.tensor_copy(dtbb[:], ps_bc[:]))

        wbufs = [wa, wb]
        wstate = [Buf(), Buf()]
        whbufs = [wha, whb]
        whstate = [Buf(), Buf()]
        evl = [ev0, ev1]
        estate = [Buf(), Buf()]
        pslt = [ps_a, ps_b]
        pstate = [Buf(), Buf()]
        xr_buf = Buf()
        dt_ps = Buf()
        it = 0

        for ti in range(NTI):
            xr_buf.guard_w(y)
            for c in range(8):
                SY.inc(y.dma_start(out=xr_t[:, c * NT:(c + 1) * NT],
                                   in_=xr[c * 128:(c + 1) * 128,
                                          ti * NT:(ti + 1) * NT]), 16)
            ldmark = SY.n
            for oc in range(34):
                wsl = wbufs[oc % 2]
                wst = wstate[oc % 2]
                wst.guard_w(g)
                SG.inc(g.dma_start(out=wsl[:], in_=ipw[oc]), 16)
                pp = pslt[oc % 2]
                pst = pstate[oc % 2]
                pst.guard_w(t)
                SG.wait(t)
                if oc == 0:
                    SY.wait_at(t, ldmark)
                for kc in range(8):
                    ins = t.matmul(pp[:], wsl[:, kc * 128:(kc + 1) * 128],
                                   xr_t[:, kc * NT:(kc + 1) * NT],
                                   start=(kc == 0), stop=(kc == 7))
                ST.inc(ins)
                wst.rd(ST)
                pst.wr(ST)
                ee = evl[oc % 2]
                est = estate[oc % 2]
                est.guard_w(s)
                pst.guard_r(s)
                SS.inc(s.activation(ee[:], pp[:], AF.Identity))
                pst.rd(SS)
                est.wr(SS)
                est.guard_r(y)
                if oc < 16:
                    dst = z_d[oc * 128:(oc + 1) * 128, ti * NT:(ti + 1) * NT]
                elif oc < 32:
                    dst = xs_d[(oc - 16) * 128:(oc - 15) * 128,
                               ti * NT:(ti + 1) * NT]
                elif oc == 32:
                    dst = B_d[:, ti * NT:(ti + 1) * NT]
                else:
                    dst = C_d[:, ti * NT:(ti + 1) * NT]
                SY.inc(y.dma_start(out=dst, in_=ee[:]), 16)
                est.rd(SY)
            xr_buf.rd(ST)
            for tsub in range(4):
                tk = ti * NT + tsub * 128
                dt_ps.guard_w(t)
                SG.wait(t)
                for kc in range(8):
                    ins = t.matmul(ps_dt[:],
                                   xr_t[:, kc * NT + tsub * 128:
                                        kc * NT + (tsub + 1) * 128],
                                   dtw_sb[:, kc * 32:(kc + 1) * 32],
                                   start=(kc == 0), stop=(kc == 7))
                ST.inc(ins)
                dslice = dsc[:, tsub * NH:(tsub + 1) * NH]
                ST.wait(v)
                SV.inc(v.tensor_tensor(dslice, ps_dt[:], dtbb[:], ALU.add))
                SV.wait(v)
                dt_ps.rd(SV)
                SV.inc(v.tensor_scalar_max(dto[:], dslice, 0.0))
                SV.wait(s)
                SS.inc(s.activation(dslice, dslice, AF.Abs))
                SS.wait(s)
                SS.inc(s.activation(dslice, dslice, AF.Exp, scale=-1.0))
                SS.wait(s)
                SS.inc(s.activation(dslice, dslice, AF.Ln,
                                    bias=k.fconst(1.0, 128)))
                SS.wait(v)
                SV.wait(v)
                SV.inc(v.tensor_tensor(dto[:], dto[:], dslice, ALU.add))
                SV.wait(y)
                SY.inc(y.dma_start(out=dtT_d[tk:tk + 128, :], in_=dto[:]), 16)
                SY.wait(v)
        xn_buf = Buf()
        t1_buf = Buf()
        t2_buf = Buf()
        for fi in range(FTI):
            xn_buf.guard_w(y)
            for c in range(8):
                SY.inc(y.dma_start(out=xnt[:, c * FT:(c + 1) * FT],
                                   in_=xn1[c * 128:(c + 1) * 128,
                                           fi * FT:(fi + 1) * FT]), 16)
            ldmark = SY.n
            for oc in range(32):
                wst = wstate[oc % 2]
                wsl = wbufs[oc % 2]
                wst.guard_w(g)
                SG.inc(g.dma_start(out=wsl[:], in_=w1i[oc]), 16)
                whst = whstate[oc % 2]
                whl = whbufs[oc % 2]
                whst.guard_w(g)
                SG.inc(g.dma_start(out=whl[:, 0:D], in_=w2i[oc]), 16)
                SG.wait(t)
                if oc == 0:
                    SY.wait_at(t, ldmark)
                t1_buf.guard_w(t)
                for nh in range(2):
                    pp = pslt[nh]
                    pst = pstate[nh]
                    pst.guard_w(t)
                    for kc in range(8):
                        ins = t.matmul(
                            pp[:], wsl[:, kc * 128:(kc + 1) * 128],
                            xnt[:, kc * FT + nh * NT:kc * FT + (nh + 1) * NT],
                            start=(kc == 0), stop=(kc == 7))
                    ST.inc(ins)
                    pst.wr(ST)
                    pp2 = ps_c if nh == 0 else ps_d
                    for kc in range(8):
                        ins = t.matmul(
                            pp2[:], whl[:, kc * 128:(kc + 1) * 128],
                            xnt[:, kc * FT + nh * NT:kc * FT + (nh + 1) * NT],
                            start=(kc == 0), stop=(kc == 7))
                    ST.inc(ins)
                wst.rd(ST)
                whst.rd(ST)
                ST.wait(s)
                for nh in range(2):
                    pp = pslt[nh]
                    pp2 = ps_c if nh == 0 else ps_d
                    s.activation(gsb[:, nh * NT:(nh + 1) * NT], pp[:],
                                 AF.Silu, scale=s1sb[:, oc:oc + 1])
                    ins = s.activation(usb[:, nh * NT:(nh + 1) * NT], pp2[:],
                                       AF.Identity, scale=s2sb[:, oc:oc + 1])
                SS.inc(ins)
                for nh in range(2):
                    pstate[nh].rd(SS)
                SS.wait(v)
                SV.inc(v.tensor_tensor(t1sb[:, oc * FT:(oc + 1) * FT],
                                       gsb[:], usb[:], ALU.mult))
            xn_buf.rd(ST)
            SV.wait(t)
            t2_buf.guard_w(t)
            for oc in range(32):
                whst = whstate[oc % 2]
                whl = whbufs[oc % 2]
                whst.guard_w(g)
                SG.inc(g.dma_start(out=whl[:], in_=h1i[oc]), 16)
                SG.wait(t)
                for nh in range(2):
                    pp = pslt[nh]
                    pst = pstate[nh]
                    pst.guard_w(t)
                    for kc in range(32):
                        ins = t.matmul(
                            pp[:], whl[:, kc * 128:(kc + 1) * 128],
                            t1sb[:, kc * FT + nh * NT:kc * FT + (nh + 1) * NT],
                            start=(kc == 0), stop=(kc == 31))
                    ST.inc(ins)
                    pst.wr(ST)
                whst.rd(ST)
                ST.wait(s)
                for nh in range(2):
                    ins = s.activation(
                        t2sb[:, oc * FT + nh * NT:oc * FT + (nh + 1) * NT],
                        pslt[nh][:], AF.Gelu, scale=sh1sb[:, oc:oc + 1])
                SS.inc(ins)
                for nh in range(2):
                    pstate[nh].rd(SS)
            SS.wait(t)
            t1_buf2 = Buf()
            for oc in range(32):
                whst = whstate[oc % 2]
                whl = whbufs[oc % 2]
                whst.guard_w(g)
                SG.inc(g.dma_start(out=whl[:], in_=h2i[oc]), 16)
                SG.wait(t)
                for nh in range(2):
                    pp = pslt[nh]
                    pst = pstate[nh]
                    pst.guard_w(t)
                    for kc in range(32):
                        ins = t.matmul(
                            pp[:], whl[:, kc * 128:(kc + 1) * 128],
                            t2sb[:, kc * FT + nh * NT:kc * FT + (nh + 1) * NT],
                            start=(kc == 0), stop=(kc == 31))
                    ST.inc(ins)
                    pst.wr(ST)
                whst.rd(ST)
                ST.wait(s)
                for nh in range(2):
                    ins = s.activation(
                        t1sb[:, oc * FT + nh * NT:oc * FT + (nh + 1) * NT],
                        pslt[nh][:], AF.Gelu, scale=sh2sb[:, oc:oc + 1])
                SS.inc(ins)
                for nh in range(2):
                    pstate[nh].rd(SS)
            t1_buf.wr(SS)
            SS.wait(t)
            for oc in range(8):
                whst = whstate[oc % 2]
                whl = whbufs[oc % 2]
                whst.guard_w(g)
                SG.inc(g.dma_start(out=whl[:], in_=w3i[oc]), 16)
                SG.wait(t)
                for nh in range(2):
                    pp = pslt[nh]
                    pst = pstate[nh]
                    pst.guard_w(t)
                    for kc in range(32):
                        ins = t.matmul(
                            pp[:], whl[:, kc * 128:(kc + 1) * 128],
                            t1sb[:, kc * FT + nh * NT:kc * FT + (nh + 1) * NT],
                            start=(kc == 0), stop=(kc == 31))
                    ST.inc(ins)
                    pst.wr(ST)
                whst.rd(ST)
                ST.wait(s)
                for nh in range(2):
                    ins = s.activation(fev[:, nh * NT:(nh + 1) * NT],
                                       pslt[nh][:], AF.Identity,
                                       scale=s3sb[:, oc:oc + 1])
                SS.inc(ins)
                for nh in range(2):
                    pstate[nh].rd(SS)
                SS.wait(y)
                SY.inc(y.dma_start(out=f1_d[oc * 128:(oc + 1) * 128,
                                            fi * FT:(fi + 1) * FT],
                                   in_=fev[:]), 16)
                SY.wait(s)
            for _sem in (SY, SV, ST, SS, SG):
                for _eng in (t, v, s, g, y):
                    _sem.wait(_eng)
        SY.wait(y)
        SG.wait(y)
    return nc


def run_p4(inputs, p3res):
    ip = inputs["L1_attn_in_proj_w"]
    def img_outer_f(w, nhc):
        o, i = w.shape
        a = w.reshape(nhc, 128, i // 128, 128)
        return np.ascontiguousarray(
            a.transpose(0, 3, 2, 1)).reshape(nhc, 128, i).astype(np.float32)
    ipw_img = img_outer_f(ip[0:4352], 34)
    dtw = np.ascontiguousarray(
        ip[4352:4384].reshape(NH, 8, 128).transpose(2, 1, 0).reshape(
            128, 256)).astype(np.float32)
    w1_img = img_outer_f(inputs["L1_ffn_w1_w"], 32)
    w2_img = img_outer_f(inputs["L1_ffn_w2_w"], 32)
    h1_img = img_outer_f(inputs["L1_ffn_h1_w"], 32)
    h2_img = img_outer_f(inputs["L1_ffn_h2_w"], 32)
    w3_img = img_outer_f(inputs["L1_ffn_w3_w"], 8)
    nc = build_p4()
    in_maps = []
    for c in CORES:
        in_maps.append({
            "xr": p3res[c]["xr"], "xn1": p3res[c]["xn1"],
            "ipw_img": ipw_img, "dtw_img": dtw,
            "w1_img": w1_img, "w2_img": w2_img, "h1_img": h1_img,
            "h2_img": h2_img, "w3_img": w3_img,
            "s1r": inputs["L1_ffn_w1_s"].reshape(HID1, 1).astype(np.float32),
            "s2r": inputs["L1_ffn_w2_s"].reshape(HID1, 1).astype(np.float32),
            "sh1r": inputs["L1_ffn_h1_s"].reshape(HID1, 1).astype(np.float32),
            "sh2r": inputs["L1_ffn_h2_s"].reshape(HID1, 1).astype(np.float32),
            "s3r": inputs["L1_ffn_w3_s"].reshape(D, 1).astype(np.float32),
            "dtb": inputs["L1_attn_dt_bias"].reshape(1, NH).astype(np.float32),
        })
    res = run_bass_kernel_spmd(nc, in_maps, core_ids=CORES)
    return res.results


def _tri_incl():
    r = np.arange(128)[:, None]
    tt = np.arange(128)[None, :]
    return (r <= tt).astype(np.float32)


def _tri_strict():
    r = np.arange(128)[:, None]
    j = np.arange(128)[None, :]
    return (r > j).astype(np.float32)


def build_p4b():
    k = K("p4b")
    nc = k.nc
    xsf = nc.dram_tensor("xsf", [DI, 2 * (BLK + 3)], BF16,
                         kind="ExternalInput").ap()
    Bfull = nc.dram_tensor("Bfull", [DS, T], BF16, kind="ExternalInput").ap()
    dtf = nc.dram_tensor("dtf", [T, NH], F32, kind="ExternalInput").ap()
    cw = nc.dram_tensor("cw", [DI, 4], F32, kind="ExternalInput").ap()
    cb = nc.dram_tensor("cb", [DI, 1], F32, kind="ExternalInput").ap()
    alog = nc.dram_tensor("alog", [1, NH], F32, kind="ExternalInput").ap()
    xh_d = nc.dram_tensor("xh", [DI, T], BF16, kind="ExternalOutput").ap()
    summ_d = nc.dram_tensor("summ", [2 * DS, DI], F32,
                            kind="ExternalOutput").ap()
    lam_d = nc.dram_tensor("lam", [2, NH], F32, kind="ExternalOutput").ap()

    xsl = k.sb("xsl", [128, BLK + 3], BF16)
    acc = k.sb("acc", [128, BLK], F32)
    xh_sb = k.sb("xh_sb", [128, 16 * BLK], BF16)
    cw_sb = k.sb("cw_sb", [128, 16 * 4], F32)
    cb_sb = k.sb("cb_sb", [128, 16], F32)
    nea = k.sb("nea", [128, NH], F32)
    arow = k.sb("arow", [1, NH], F32)
    dtt = k.sb("dtt", [128, NH], F32)
    bsl = k.sb("bsl", [128, 128], BF16)
    la_sb = k.sb("la_sb", [128, NH], F32)
    db_sb = k.sb("db_sb", [128, NH], F32)
    dtdb = k.sb("dtdb", [128, NH], F32)
    lsum = k.sb("lsum", [1, NH], F32)
    lrow = k.sb("lrow", [1, NH], F32)
    lamb = k.sb("lamb", [128, NH], F32)
    lamo = k.sb("lamo", [1, NH], F32)
    xh_tm = k.sb("xh_tm", [128, DI], BF16)
    b_tm = k.sb("b_tm", [128, 128], BF16)
    x2_tm = k.sb("x2_tm", [128, DI], BF16)
    hs = k.sb("hs", [128, DI], F32)
    ps_misc = k.ps("ps_misc", [128, 128])
    ps_tr = k.ps("ps_tr", [128, 1024], BF16)
    ps_ds = k.ps("ps_ds", [128, DI])
    ps_row = k.ps("ps_row", [1, NH])

    with nc.Block() as _blk:
        t, v, s, g, y = k.t, k.v, k.s, k.g, k.y
        SY, SG, ST, SV, SS = (k.sem(n) for n in ("SY", "SG", "ST", "SV", "SS"))
        U = k.const("Utri", _tri_incl())
        Vs = k.const("Vstrict", _tri_strict())
        idb = k.ident_bf
        ones_colf = k.const("ones_colf", np.ones((128, 1), dtype=np.float32))
        _ = k.ones_row_f32
        k.cwait(t)
        k.cwait(v)
        k.cwait(s)
        for fc in range(16):
            SY.inc(y.dma_start(out=cw_sb[:, fc * 4:(fc + 1) * 4],
                               in_=cw[fc * 128:(fc + 1) * 128, :]), 16)
            SY.inc(y.dma_start(out=cb_sb[:, fc:fc + 1],
                               in_=cb[fc * 128:(fc + 1) * 128, :]), 16)
        SY.inc(y.dma_start(out=arow[:], in_=alog), 16)
        SY.wait(s)
        SS.inc(s.activation(arow[:], arow[:], AF.Exp))
        SS.wait(t)
        ST.inc(t.matmul(ps_misc[:, 0:NH], k.ones_row_f32, arow[:],
                        start=True, stop=True))
        ST.wait(v)
        SV.inc(v.tensor_scalar_mul(nea[:], ps_misc[:, 0:NH], -1.0))
        SV.wait(v)

        xsl_buf = Buf()
        acc_buf = Buf()
        dt_buf = Buf()
        b_buf = Buf()
        xhs_buf = Buf()
        for bi in range(2):
            s0 = bi * (BLK + 3)
            g0 = bi * BLK
            xhs_buf.guard_w(v)
            for fc in range(16):
                xsl_buf.guard_w(y)
                SY.inc(y.dma_start(out=xsl[:, 0:BLK + 3],
                                   in_=xsf[fc * 128:(fc + 1) * 128,
                                           s0:s0 + BLK + 3]), 16)
                SY.wait(v)
                acc_buf.guard_w(v)
                SV.inc(v.tensor_scalar_mul(acc[:], xsl[:, 0:BLK],
                                           cw_sb[:, fc * 4:fc * 4 + 1]))
                for kk in range(1, 4):
                    SV.wait(v)
                    SV.inc(v.scalar_tensor_tensor(
                        acc[:], xsl[:, kk:kk + BLK],
                        cw_sb[:, fc * 4 + kk:fc * 4 + kk + 1],
                        acc[:], ALU.mult, ALU.add))
                xsl_buf.rd(SV)
                SV.wait(s)
                ins = s.activation(xh_sb[:, fc * BLK:(fc + 1) * BLK],
                                   acc[:], AF.Silu, bias=cb_sb[:, fc:fc + 1])
                SS.inc(ins)
                acc_buf.rd(SS)
            SS.wait(y)
            for fc in range(16):
                SY.inc(y.dma_start(
                    out=xh_d[fc * 128:(fc + 1) * 128,
                             bi * BLK:(bi + 1) * BLK],
                    in_=xh_sb[:, fc * BLK:(fc + 1) * BLK]), 16)
            SV.wait(g)
            SS.wait(g)
            SG.inc(g.memset(hs[:], 0.0))
            SG.inc(g.memset(lsum[:], 0.0))
            SG.wait(v)
            SG.wait(t)
            SS.wait(t)
            for ch in range(16):
                tk = g0 + ch * 128
                dt_buf.guard_w(y)
                SY.inc(y.dma_start(out=dtt[:], in_=dtf[tk:tk + 128, :]), 16)
                b_buf.guard_w(y)
                SY.inc(y.dma_start(out=bsl[:],
                                   in_=Bfull[:, tk:tk + 128]), 16)
                SY.wait(v)
                SV.wait(v)
                SV.inc(v.tensor_tensor(la_sb[:], dtt[:], nea[:], ALU.mult))
                SV.wait(v)
                SV.wait(t)
                ST.wait(v)
                ins = t.matmul(ps_misc[:, 0:NH], Vs, la_sb[:],
                               start=True, stop=True)
                ST.inc(t.matmul(ps_row[:], ones_colf, la_sb[:],
                                start=True, stop=True))
                ST.wait(s)
                SS.inc(s.activation(db_sb[:], ps_misc[:, 0:NH], AF.Exp))
                ST.wait(v)
                SV.inc(v.tensor_copy(lrow[:], ps_row[:]))
                SV.wait(v)
                SV.inc(v.tensor_tensor(lsum[:], lsum[:], lrow[:], ALU.add))
                SS.wait(v)
                SV.inc(v.tensor_tensor(dtdb[:], dtt[:], db_sb[:], ALU.mult))
                SV.wait(v)
                dt_buf.rd(SV)
                SV.wait(t)
                ST.inc(t.matmul(ps_misc[:, 32:64], k.ones_row_f32, lrow[:],
                                start=True, stop=True))
                ST.wait(s)
                SS.inc(s.activation(lamb[:], ps_misc[:, 32:64], AF.Exp))
                SY.wait(t)
                waves = [[("b", 0)] + [("x", hp) for hp in range(7)],
                         [("x", hp) for hp in range(7, 15)],
                         [("x", 15)]]
                for wv in waves:
                    SS.wait(t)
                    for slot, (kind, hp) in enumerate(wv):
                        if kind == "b":
                            ins = t.transpose(ps_tr[:, slot * 128:
                                                    (slot + 1) * 128],
                                              bsl[:], idb)
                        else:
                            ins = t.transpose(
                                ps_tr[:, slot * 128:(slot + 1) * 128],
                                xh_sb[:, hp * BLK + ch * 128:
                                      hp * BLK + ch * 128 + 128], idb)
                    ST.inc(ins)
                    ST.wait(s)
                    for slot, (kind, hp) in enumerate(wv):
                        dst = b_tm[:] if kind == "b" else \
                            xh_tm[:, hp * 128:(hp + 1) * 128]
                        ins = s.activation(dst,
                                           ps_tr[:, slot * 128:
                                                 (slot + 1) * 128],
                                           AF.Identity)
                    SS.inc(ins)
                b_buf.rd(ST)
                xhs_buf.rd(ST)
                SS.wait(v)
                for h in range(NH):
                    ins = v.tensor_scalar_mul(
                        x2_tm[:, h * HD:(h + 1) * HD],
                        xh_tm[:, h * HD:(h + 1) * HD], dtdb[:, h:h + 1])
                SV.inc(ins)
                SV.wait(t)
                SS.wait(t)
                for h in range(NH):
                    ins = t.matmul(ps_ds[:, h * HD:(h + 1) * HD], b_tm[:],
                                   x2_tm[:, h * HD:(h + 1) * HD],
                                   start=True, stop=True)
                ST.inc(ins)
                ST.wait(v)
                SS.wait(v)
                for h in range(NH):
                    ins = v.tensor_scalar_mul(hs[:, h * HD:(h + 1) * HD],
                                              hs[:, h * HD:(h + 1) * HD],
                                              lamb[:, h:h + 1])
                SV.inc(ins)
                SV.wait(v)
                for h in range(NH):
                    ins = v.tensor_tensor(hs[:, h * HD:(h + 1) * HD],
                                          hs[:, h * HD:(h + 1) * HD],
                                          ps_ds[:, h * HD:(h + 1) * HD],
                                          ALU.add)
                SV.inc(ins)
                SV.wait(v)
            SV.wait(y)
            for half in range(2):
                SY.inc(y.dma_start(
                    out=summ_d[bi * DS:(bi + 1) * DS,
                               half * 1024:(half + 1) * 1024],
                    in_=hs[:, half * 1024:(half + 1) * 1024]), 16)
            SV.wait(s)
            SS.inc(s.activation(lamo[:], lsum[:], AF.Exp))
            SS.wait(y)
            SY.inc(y.dma_start(out=lam_d[bi:bi + 1, :], in_=lamo[:]), 16)
            SY.wait(v)
            SY.wait(g)
            for _sem in (SY, SV, ST, SS, SG):
                for _eng in (t, v, s, g, y):
                    _sem.wait(_eng)
        SY.wait(y)
        SG.wait(y)
    return nc


def run_p4b(inputs, p3res, p4res):
    xsfull = np.zeros((B, DI, S), dtype=BF)
    bfull = np.zeros((B, DS, S), dtype=BF)
    dtfull = np.zeros((B, S, NH), dtype=np.float32)
    for c in CORES:
        b = c // 4
        _scatter_tokens(xsfull[b], p4res[c]["xs"], c, axis=1)
        _scatter_tokens(bfull[b], p4res[c]["Bf"], c, axis=1)
        _scatter_tokens(dtfull[b], p4res[c]["dtT"], c, axis=0)
    nc4b = build_p4b()
    in_maps = []
    for c in CORES:
        b = c // 4
        xsp = np.zeros((DI, 2 * (BLK + 3)), dtype=BF)
        for bi, blk in enumerate(blocks_of(c)):
            s0 = bi * (BLK + 3)
            if blk > 0:
                xsp[:, s0:s0 + 3] = xsfull[b][:, blk * BLK - 3:blk * BLK]
            xsp[:, s0 + 3:s0 + 3 + BLK] = \
                xsfull[b][:, blk * BLK:(blk + 1) * BLK]
        in_maps.append({
            "xsf": xsp, "Bfull": p4res[c]["Bf"], "dtf": p4res[c]["dtT"],
            "cw": inputs["L1_attn_conv_w"].astype(np.float32),
            "cb": inputs["L1_attn_conv_b"].reshape(DI, 1).astype(
                np.float32),
            "alog": inputs["L1_attn_A_log"].reshape(1, NH).astype(
                np.float32)})
    r = run_bass_kernel_spmd(nc4b, in_maps, core_ids=CORES)
    return list(r.results)


def _upper_incl_bf():
    jj = np.arange(128)[:, None]
    ii = np.arange(128)[None, :]
    return (ii >= jj).astype(BF)


def _exp32(g):
    out = np.zeros((32, 16 * 128), dtype=np.float32)
    for hh in range(16):
        out[g * 16 + hh, hh * 128:(hh + 1) * 128] = 1.0
    return out


def build_p5():
    k = K("p5")
    nc = k.nc
    summ_a = nc.dram_tensor("summ_a", [14 * DS, DI], F32,
                            kind="ExternalInput").ap()
    lam_a = nc.dram_tensor("lam_a", [14, NH], F32, kind="ExternalInput").ap()
    xh_d = nc.dram_tensor("xh", [DI, T], BF16, kind="ExternalInput").ap()
    z_d = nc.dram_tensor("z", [DI, T], BF16, kind="ExternalInput").ap()
    Bf = nc.dram_tensor("Bf", [DS, T], BF16, kind="ExternalInput").ap()
    Cf = nc.dram_tensor("Cf", [DS, T], BF16, kind="ExternalInput").ap()
    dtT = nc.dram_tensor("dtT", [T, NH], F32, kind="ExternalInput").ap()
    x1_d = nc.dram_tensor("x1", [D, T], F32, kind="ExternalInput").ap()
    st1_d = nc.dram_tensor("st1", [2, T], F32, kind="ExternalInput").ap()
    f1_d = nc.dram_tensor("f1", [D, T], F32, kind="ExternalInput").ap()
    opi = nc.dram_tensor("op_img", [16, 128, D], F32, kind="ExternalInput").ap()
    alog = nc.dram_tensor("alog", [1, NH], F32, kind="ExternalInput").ap()
    dexp = nc.dram_tensor("dexp", [DI, 1], F32, kind="ExternalInput").ap()
    x2_d = nc.dram_tensor("x2", [D, T], F32, kind="ExternalOutput").ap()

    NT = 512
    op_sb = k.sb("op_sb", [128, 16 * D], BF16)
    xh_sb = k.sb("xh_sb", [128, 16 * NT], BF16)
    hs = k.sb("hs", [128, DI], F32)
    hs_bf = k.sb("hs_bf", [128, DI], BF16)
    s_tile = k.sb("s_tile", [128, 1024], F32)
    lam_sb = k.sb("lam_sb", [1, 14 * NH], F32)
    lamb = k.sb("lamb", [128, NH], F32)
    nea = k.sb("nea", [128, NH], F32)
    arow = k.sb("arow", [1, NH], F32)
    dtt = k.sb("dtt", [128, NH], F32)
    bsl = k.sb("bsl", [128, 128], BF16)
    csl = k.sb("csl", [128, 128], BF16)
    la_sb = k.sb("la_sb", [128, NH], F32)
    db_sb = k.sb("db_sb", [128, NH], F32)
    dtdb = k.sb("dtdb", [128, NH], F32)
    naf = k.sb("naf", [32, 128], F32)
    af = k.sb("af", [32, 128], F32)
    amask = k.sb("amask", [32, 2048], F32)
    lrow_e = k.sb("lrow_e", [1, NH], F32)
    cbc = k.sb("cbc", [128, 128], BF16)
    E_sb = k.sb("E_sb", [128, 2048], BF16)
    Ce_sb = k.sb("Ce_sb", [128, 2048], BF16)
    esc = k.sb("esc", [128, 512], F32)
    b_tm = k.sb("b_tm", [128, 128], BF16)
    xh_tm = k.sb("xh_tm", [128, DI], BF16)
    xt_tm = k.sb("xt_tm", [128, DI], BF16)
    x2_tm = k.sb("x2_tm", [128, DI], BF16)
    dexp_sb = k.sb("dexp_sb", [128, 16], F32)
    y_acc = k.sb("y_acc", [128, 16 * NT], BF16)
    silz = k.sb("silz", [128, 16 * NT], BF16)
    base = k.sb("base", [128, 8 * NT], F32)
    xft = k.sb("xft", [128, 8 * NT], F32)
    f1t = k.sb("f1t", [128, 8 * NT], F32)
    strow = k.sb("strow", [1, 3 * NT], F32)
    ps_misc = k.ps("ps_misc", [128, 512])
    ps_tr = k.ps("ps_tr", [128, 1024], BF16)
    ps_big = k.ps("ps_big", [128, 2048])
    ps_y0 = k.ps("ps_y0", [128, 128])
    ps_y1 = k.ps("ps_y1", [128, 128])

    with nc.Block() as _blk:
        t, v, s, g, y = k.t, k.v, k.s, k.g, k.y
        SY, SG, ST, SV, SS = (k.sem(n) for n in ("SY", "SG", "ST", "SV", "SS"))
        U = k.const("Utri", _tri_incl())
        Vs = k.const("Vstrict", _tri_strict())
        UPm = k.const("uppermask", _upper_incl_bf())
        EXg = [k.const("exp32_0", _exp32(0)), k.const("exp32_1", _exp32(1))]
        ones32 = k.const("ones32", np.ones((32, 128), dtype=np.float32))
        ones_colf = k.const("ones_colf", np.ones((128, 1), dtype=np.float32))
        idb = k.ident_bf
        _ = k.ones_row_f32
        k.cwait(t)
        k.cwait(v)
        k.cwait(s)
        for kc in range(16):
            SG.inc(g.dma_start(out=op_sb[:, kc * D:(kc + 1) * D],
                               in_=opi[kc]), 16)
        SY.inc(y.dma_start(out=lam_sb[:],
                           in_=lam_a.rearrange("a b -> (a b)")[None, :]),
               16)
        for fc in range(16):
            SY.inc(y.dma_start(out=dexp_sb[:, fc:fc + 1],
                               in_=dexp[fc * 128:(fc + 1) * 128, :]), 16)
        SY.inc(y.dma_start(out=arow[:], in_=alog), 16)
        SY.wait(t)
        ST.inc(t.matmul(ps_misc[:, 224:224 + NH], k.ones_row_f32,
                        arow[:], start=True, stop=True))
        ST.wait(s)
        SS.inc(s.activation(nea[:], ps_misc[:, 224:224 + NH], AF.Exp))
        SS.wait(v)
        SV.inc(v.tensor_scalar_mul(nea[:], nea[:], -1.0))
        SV.wait(v)
        SS.wait(t)

        bufs = {n: Buf() for n in ("dt", "b", "c", "xh", "z", "st", "x1",
                                   "f1", "ya", "yg", "base", "hsb", "stile")}
        for bi in range(2):
            SV.wait(g)
            SG.inc(g.memset(hs[:], 0.0))
            SG.wait(v)
            for m in range(7):
                me = bi * 7 + m
                SV.wait(t)
                ST.inc(t.matmul(ps_misc[:, 32:64], k.ones_row_f32,
                                lam_sb[0:1, me * NH:(me + 1) * NH],
                                start=True, stop=True))
                ST.wait(v)
                SV.inc(v.tensor_copy(lamb[:], ps_misc[:, 32:64]))
                for half in range(2):
                    bufs["stile"].guard_w(y)
                    SY.inc(y.dma_start(
                        out=s_tile[:],
                        in_=summ_a[me * DS:(me + 1) * DS,
                                   half * 1024:(half + 1) * 1024]), 16)
                    SY.wait(v)
                    SV.wait(v)
                    for h in range(16):
                        hh = half * 16 + h
                        ins = v.scalar_tensor_tensor(
                            hs[:, hh * HD:(hh + 1) * HD],
                            hs[:, hh * HD:(hh + 1) * HD],
                            lamb[:, hh:hh + 1],
                            s_tile[:, h * HD:(h + 1) * HD],
                            ALU.mult, ALU.add)
                    SV.inc(ins)
                    bufs["stile"].rd(SV)
                SV.wait(v)
            bufs["hsb"].guard_w(v)
            SV.inc(v.tensor_copy(hs_bf[:], hs[:]))
            SV.wait(v)
            for cg in range(4):
                t0 = bi * BLK + cg * NT
                bufs["xh"].guard_w(y)
                for fc in range(16):
                    SY.inc(y.dma_start(
                        out=xh_sb[:, fc * NT:(fc + 1) * NT],
                        in_=xh_d[fc * 128:(fc + 1) * 128, t0:t0 + NT]), 16)
                xhmark = SY.n
                bufs["ya"].guard_w(v)
                for ch4 in range(4):
                    ch = cg * 4 + ch4
                    tk = t0 + ch4 * 128
                    bufs["dt"].guard_w(y)
                    SY.inc(y.dma_start(out=dtt[:], in_=dtT[tk:tk + 128, :]),
                           16)
                    bufs["b"].guard_w(y)
                    SY.inc(y.dma_start(out=bsl[:], in_=Bf[:, tk:tk + 128]),
                           16)
                    bufs["c"].guard_w(y)
                    SY.inc(y.dma_start(out=csl[:], in_=Cf[:, tk:tk + 128]),
                           16)
                    SY.wait(v)
                    SV.wait(v)
                    SV.inc(v.tensor_tensor(la_sb[:], dtt[:], nea[:],
                                           ALU.mult))
                    SV.wait(v)
                    SV.wait(t)
                    ST.wait(v)
                    SS.wait(t)
                    t.matmul(ps_misc[0:32, 64:192], la_sb[:], U,
                             start=True, stop=True)
                    t.matmul(ps_misc[:, 0:32], Vs, la_sb[:],
                             start=True, stop=True)
                    ST.inc(t.matmul(ps_misc[0:1, 192:224], ones_colf,
                                    la_sb[:], start=True, stop=True))
                    ST.wait(s)
                    s.activation(db_sb[:], ps_misc[:, 0:32], AF.Exp)
                    SS.inc(s.activation(lrow_e[:], ps_misc[0:1, 192:224],
                                        AF.Exp))
                    ST.wait(v)
                    SS.wait(v)
                    v.tensor_scalar_mul(naf[:], ps_misc[0:32, 64:192], -1.0)
                    SV.inc(v.tensor_copy(af[:], ps_misc[0:32, 64:192]))
                    SV.inc(v.tensor_tensor(dtdb[:], dtt[:], db_sb[:],
                                           ALU.mult))
                    SV.wait(v)
                    SV.wait(t)
                    SS.wait(t)
                    ST.inc(t.matmul(ps_misc[:, 32:64], k.ones_row_f32,
                                    lrow_e[:], start=True, stop=True))
                    ST.wait(v)
                    SV.inc(v.tensor_copy(lamb[:], ps_misc[:, 32:64]))
                    SV.wait(t)
                    ST.inc(t.matmul(ps_big[:, 0:128], bsl[:], csl[:],
                                    start=True, stop=True))
                    ST.wait(v)
                    SV.inc(v.tensor_tensor(cbc[:], ps_big[:, 0:128], UPm,
                                           ALU.mult))
                    SV.wait(v)
                    pyb_g = [Buf(), Buf()]
                    if SY.n > xhmark - 1:
                        SY.wait_at(t, xhmark)
                    waves = [[("b", 0)] + [("x", hp) for hp in range(7)],
                             [("x", hp) for hp in range(7, 15)],
                             [("x", 15)]]
                    for wv in waves:
                        SS.wait(t)
                        for slot, (kind, hp) in enumerate(wv):
                            if kind == "b":
                                ins = t.transpose(
                                    ps_tr[:, slot * 128:(slot + 1) * 128],
                                    bsl[:], idb)
                            else:
                                ins = t.transpose(
                                    ps_tr[:, slot * 128:(slot + 1) * 128],
                                    xh_sb[:, hp * NT + ch4 * 128:
                                          hp * NT + ch4 * 128 + 128], idb)
                        ST.inc(ins)
                        ST.wait(s)
                        for slot, (kind, hp) in enumerate(wv):
                            dst = b_tm[:] if kind == "b" else \
                                xh_tm[:, hp * 128:(hp + 1) * 128]
                            ins = s.activation(
                                dst, ps_tr[:, slot * 128:(slot + 1) * 128],
                                AF.Identity)
                        SS.inc(ins)
                    bufs["b"].rd(ST)
                    SS.wait(v)
                    for h in range(NH):
                        v.tensor_scalar_mul(xt_tm[:, h * HD:(h + 1) * HD],
                                            xh_tm[:, h * HD:(h + 1) * HD],
                                            dtt[:, h:h + 1])
                        ins = v.tensor_scalar_mul(
                            x2_tm[:, h * HD:(h + 1) * HD],
                            xh_tm[:, h * HD:(h + 1) * HD], dtdb[:, h:h + 1])
                    SV.inc(ins)
                    bufs["dt"].rd(SV)
                    SV.wait(t)
                    def run_pairs(gi, ch4=ch4):
                        SV.wait(t)
                        pys = [ps_y0, ps_y1]
                        pybufs = [pyb_g[0], pyb_g[1]]
                        for p in range(gi * 8, gi * 8 + 8):
                            py = pys[p % 2]
                            pyb = pybufs[p % 2]
                            pyb.guard_w(t)
                            for hh in range(2):
                                h = 2 * p + hh
                                slot = h % 16
                                t.matmul(py[hh * 64:hh * 64 + 64, :],
                                         xt_tm[:, h * HD:(h + 1) * HD],
                                         E_sb[:, slot * 128:(slot + 1) * 128],
                                         start=True, stop=False)
                                ins = t.matmul(
                                    py[hh * 64:hh * 64 + 64, :],
                                    hs_bf[:, h * HD:(h + 1) * HD],
                                    Ce_sb[:, slot * 128:(slot + 1) * 128],
                                    start=False, stop=True)
                            ST.inc(ins)
                            ST.wait(v)
                            ins = v.scalar_tensor_tensor(
                                y_acc[:, p * NT + ch4 * 128:
                                      p * NT + ch4 * 128 + 128],
                                xh_sb[:, p * NT + ch4 * 128:
                                      p * NT + ch4 * 128 + 128],
                                dexp_sb[:, p:p + 1], py[:],
                                ALU.mult, ALU.add)
                            SV.inc(ins)
                            pyb.rd(SV)
                    for gi in range(2):
                        arep = af.rearrange("p (o n) -> p o n", o=1)\
                            .broadcast_to([32, 16, 128])
                        SV.inc(v.tensor_tensor(
                            amask.rearrange("p (o n) -> p o n", o=16),
                            EXg[gi].rearrange("p (o n) -> p o n", o=16),
                            arep, ALU.mult))
                        SV.wait(t)
                        SS.wait(t)
                        for q4 in range(4):
                            t.matmul(ps_big[:, q4 * 512:(q4 + 1) * 512],
                                     naf[:], EXg[gi][:, q4 * 512:
                                                     (q4 + 1) * 512],
                                     start=True, stop=False)
                            ins = t.matmul(
                                ps_big[:, q4 * 512:(q4 + 1) * 512],
                                ones32, amask[:, q4 * 512:(q4 + 1) * 512],
                                start=False, stop=True)
                        ST.inc(ins)
                        for q4 in range(4):
                            ST.wait(v)
                            SV.inc(v.tensor_scalar_min(
                                esc[:], ps_big[:, q4 * 512:(q4 + 1) * 512],
                                0.0))
                            SV.wait(s)
                            ins = s.activation(
                                E_sb[:, q4 * 512:(q4 + 1) * 512],
                                esc[:], AF.Exp)
                            SS.inc(ins)
                            SS.wait(v)
                        crep = cbc.rearrange("p (o n) -> p o n", o=1)\
                            .broadcast_to([128, 16, 128])
                        SS.wait(v)
                        SV.inc(v.tensor_tensor(
                            E_sb.rearrange("p (o n) -> p o n", o=16),
                            E_sb.rearrange("p (o n) -> p o n", o=16),
                            crep, ALU.mult))
                        SV.wait(v)
                        SV.wait(t)
                        for q4 in range(4):
                            ins = t.matmul(
                                ps_big[:, q4 * 512:(q4 + 1) * 512],
                                ones32, amask[:, q4 * 512:(q4 + 1) * 512],
                                start=True, stop=True)
                        ST.inc(ins)
                        ST.wait(s)
                        SS.inc(s.activation(
                            Ce_sb[:], ps_big[:, 0:2048], AF.Exp))
                        SS.wait(v)
                        cerep = csl.rearrange("p (o n) -> p o n", o=1)\
                            .broadcast_to([128, 16, 128])
                        SV.inc(v.tensor_tensor(
                            Ce_sb.rearrange("p (o n) -> p o n", o=16),
                            Ce_sb.rearrange("p (o n) -> p o n", o=16),
                            cerep, ALU.mult))
                        SV.wait(v)
                        SS.wait(t)
                        run_pairs(gi)
                    bufs["c"].rd(SV)
                    SV.wait(t)
                    SS.wait(t)
                    for h in range(NH):
                        ins = t.matmul(ps_big[:, h * HD:(h + 1) * HD],
                                       b_tm[:],
                                       x2_tm[:, h * HD:(h + 1) * HD],
                                       start=True, stop=True)
                    ST.inc(ins)
                    ST.wait(v)
                    for h in range(NH):
                        ins = v.tensor_scalar_mul(
                            hs[:, h * HD:(h + 1) * HD],
                            hs[:, h * HD:(h + 1) * HD], lamb[:, h:h + 1])
                    SV.inc(ins)
                    SV.wait(v)
                    for h in range(NH):
                        ins = v.tensor_tensor(hs[:, h * HD:(h + 1) * HD],
                                              hs[:, h * HD:(h + 1) * HD],
                                              ps_big[:, h * HD:(h + 1) * HD],
                                              ALU.add)
                    SV.inc(ins)
                    SV.wait(v)
                    bufs["hsb"].guard_w(v)
                    SV.inc(v.tensor_copy(hs_bf[:], hs[:]))
                    SV.wait(v)
                bufs["xh"].rd(SV)
                bufs["z"].guard_w(y)
                for fc in range(16):
                    SY.inc(y.dma_start(
                        out=silz[:, fc * NT:(fc + 1) * NT],
                        in_=z_d[fc * 128:(fc + 1) * 128, t0:t0 + NT]), 16)
                SY.wait(s)
                for fc in range(16):
                    ins = s.activation(silz[:, fc * NT:(fc + 1) * NT],
                                       silz[:, fc * NT:(fc + 1) * NT],
                                       AF.Silu)
                SS.inc(ins)
                bufs["z"].rd(SS)
                SS.wait(v)
                bufs["yg"].guard_w(v)
                for fc in range(16):
                    ins = v.tensor_tensor(silz[:, fc * NT:(fc + 1) * NT],
                                          y_acc[:, fc * NT:(fc + 1) * NT],
                                          silz[:, fc * NT:(fc + 1) * NT],
                                          ALU.mult)
                SV.inc(ins)
                bufs["ya"].rd(SV)
                SV.wait(v)
                bufs["x1"].guard_w(y)
                for c_ in range(8):
                    SY.inc(y.dma_start(
                        out=xft[:, c_ * NT:(c_ + 1) * NT],
                        in_=x1_d[c_ * 128:(c_ + 1) * 128, t0:t0 + NT]), 16)
                    SY.inc(y.dma_start(
                        out=f1t[:, c_ * NT:(c_ + 1) * NT],
                        in_=f1_d[c_ * 128:(c_ + 1) * 128, t0:t0 + NT]), 16)
                bufs["st"].guard_w(y)
                SY.inc(y.dma_start(out=strow[0:1, 0:NT],
                                   in_=st1_d[0:1, t0:t0 + NT]), 16)
                SY.inc(y.dma_start(out=strow[0:1, NT:2 * NT],
                                   in_=st1_d[1:2, t0:t0 + NT]), 16)
                SY.wait(v)
                SV.inc(v.tensor_tensor(strow[0:1, 2 * NT:3 * NT],
                                       strow[0:1, 0:NT],
                                       strow[0:1, NT:2 * NT], ALU.mult))
                SV.wait(t)
                SS.wait(t)
                t.matmul(ps_big[:, 512:1024], k.ones_row_f32,
                         strow[0:1, NT:2 * NT], start=True, stop=True)
                ST.inc(t.matmul(ps_big[:, 1024:1536], k.ones_row_f32,
                                strow[0:1, 2 * NT:3 * NT],
                                start=True, stop=True))
                bufs["st"].rd(ST)
                ST.wait(v)
                bufs["base"].guard_w(v)
                for c_ in range(8):
                    ins = v.tensor_tensor(base[:, c_ * NT:(c_ + 1) * NT],
                                          xft[:, c_ * NT:(c_ + 1) * NT],
                                          ps_big[:, 512:1024], ALU.mult)
                SV.inc(ins)
                SV.wait(v)
                for c_ in range(8):
                    ins = v.tensor_tensor(base[:, c_ * NT:(c_ + 1) * NT],
                                          base[:, c_ * NT:(c_ + 1) * NT],
                                          ps_big[:, 1024:1536], ALU.subtract)
                SV.inc(ins)
                SV.wait(v)
                for c_ in range(8):
                    ins = v.tensor_tensor(base[:, c_ * NT:(c_ + 1) * NT],
                                          base[:, c_ * NT:(c_ + 1) * NT],
                                          xft[:, c_ * NT:(c_ + 1) * NT],
                                          ALU.add)
                SV.inc(ins)
                SV.wait(v)
                for c_ in range(8):
                    ins = v.tensor_tensor(base[:, c_ * NT:(c_ + 1) * NT],
                                          base[:, c_ * NT:(c_ + 1) * NT],
                                          f1t[:, c_ * NT:(c_ + 1) * NT],
                                          ALU.add)
                SV.inc(ins)
                bufs["x1"].rd(SV)
                SV.wait(v)
                SV.wait(t)
                SG.wait(t)
                opb = [Buf(), Buf()]
                for oc in range(8):
                    pslice = ps_big[:, (oc % 2) * 512:(oc % 2) * 512 + 512]
                    opb[oc % 2].guard_w(t)
                    for kc in range(16):
                        ins = t.matmul(
                            pslice, op_sb[:, kc * D + oc * 128:
                                          kc * D + (oc + 1) * 128],
                            silz[:, kc * NT:(kc + 1) * NT],
                            start=(kc == 0), stop=(kc == 15))
                    ST.inc(ins)
                    ST.wait(v)
                    ins = v.tensor_tensor(xft[:, oc * NT:(oc + 1) * NT],
                                          base[:, oc * NT:(oc + 1) * NT],
                                          pslice, ALU.add)
                    SV.inc(ins)
                    opb[oc % 2].rd(SV)
                    SV.wait(y)
                    SY.inc(y.dma_start(
                        out=x2_d[oc * 128:(oc + 1) * 128, t0:t0 + NT],
                        in_=xft[:, oc * NT:(oc + 1) * NT]), 16)
                bufs["yg"].rd(ST)
                bufs["z"].rd(ST)
                bufs["base"].rd(SV)
                bufs["x1"].rd(SY)
                for sem in (SY, SV, ST, SS, SG):
                    for eng in (t, v, s, g, y):
                        sem.wait(eng)
        SY.wait(y)
        SG.wait(y)
    return nc


def run_p5(inputs, p2res, p3res, p4res, p4bres):
    summs = np.zeros((B, NBLK * DS, DI), dtype=np.float32)
    lams = np.zeros((B, NBLK, NH), dtype=np.float32)
    for c in CORES:
        b = c // 4
        for bi, blk in enumerate(blocks_of(c)):
            summs[b][blk * DS:(blk + 1) * DS] = \
                p4bres[c]["summ"][bi * DS:(bi + 1) * DS]
            lams[b][blk] = p4bres[c]["lam"][bi]
    op = inputs["L1_attn_out_proj_w"]
    opT = np.ascontiguousarray(op.T)
    op_img = np.ascontiguousarray(
        opT.reshape(16, 128, 1024)).astype(np.float32)
    dexp = np.repeat(inputs["L1_attn_D"].astype(np.float32), HD).reshape(DI, 1)
    nc5 = build_p5()
    in_maps = []
    for c in CORES:
        b = c // 4
        sp = np.zeros((14 * DS, DI), np.float32)
        lp = np.ones((14, NH), np.float32)
        for bi, blk in enumerate(blocks_of(c)):
            for m in range(blk):
                sp[(bi * 7 + m) * DS:(bi * 7 + m + 1) * DS] = \
                    summs[b][m * DS:(m + 1) * DS]
                lp[bi * 7 + m] = lams[b][m]
        in_maps.append({
            "summ_a": sp, "lam_a": lp,
            "xh": p4bres[c]["xh"], "z": p4res[c]["z"],
            "Bf": p4res[c]["Bf"], "Cf": p4res[c]["Cf"],
            "dtT": p4res[c]["dtT"], "x1": p3res[c]["x1"],
            "st1": p3res[c]["st1"], "f1": p4res[c]["f1"],
            "op_img": op_img,
            "alog": inputs["L1_attn_A_log"].reshape(1, NH).astype(
                np.float32),
            "dexp": dexp,
        })
    r = run_bass_kernel_spmd(nc5, in_maps, core_ids=CORES)
    return list(r.results)
